# version 01 (final)
# baseline (speedup 1.0000x reference)
import contextlib
import numpy as np
import concourse.bass as bass
import concourse.mybir as mybir
from concourse.bass_utils import run_bass_kernel_spmd

F32 = mybir.dt.float32
BF16 = mybir.dt.bfloat16
I32 = mybir.dt.int32
AF = mybir.ActivationFunctionType
ALU = mybir.AluOpType
AX = mybir.AxisListType

D = 4096
DC = 32
H = 32
T = 2048
TT = 16
TCH = 4
FF = 11008
FC = 86
EPS = 1e-5
ISQ = float(1.0 / np.sqrt(128.0))
INV_D = float(1.0 / D)
VB = 4096 * 2048
RB = 8192 * 2048

ENGINES = ("sync", "vector", "scalar", "tensor", "gpsimd")


class Prog:
    def __init__(self):
        self.q = {e: [] for e in ENGINES}
        self.cnt = {}
        self.h = {}
        self.wm = {}

    def sem(self, name, handle):
        self.h[name] = handle
        self.cnt[name] = 0

    def emit(self, eng, fn):
        self.q[eng].append(fn)

    def wait(self, eng, sname, val):
        if val <= 0:
            return
        key = (eng, sname)
        if self.wm.get(key, 0) >= val:
            return
        self.wm[key] = val
        hh = self.h[sname]
        self.emit(eng, lambda e, hh=hh, v=val: e.wait_ge(hh, v))

    def inc(self, sname, by):
        self.cnt[sname] += by
        return self.cnt[sname]

    def do(self, eng, sname, by, fn):
        v = self.inc(sname, by)
        hh = self.h[sname]
        self.emit(eng, lambda e, fn=fn, hh=hh, by=by: fn(e).then_inc(hh, by))
        return v

    def barrier(self):
        snap = dict(self.cnt)
        for e in ENGINES:
            for s, v in snap.items():
                self.wait(e, s, v)


class Col:

    def __init__(self, handle, off, n):
        self.hd, self.off, self.n = handle, off, n
        self.w = handle.shape[1]

    def __call__(self, a=0, b=None):
        b = self.n if b is None else b
        return self.hd[:, self.off + a:self.off + b]

    def ap(self, extra, off=0):
        return bass.AP(self.hd, self.off + off, [[self.w, 128]] + extra)


def build(upto="all", debug=False):
    nc = bass.Bass("TRN2", target_bir_lowering=False)

    x = nc.dram_tensor("x", [T, D], F32, kind="ExternalInput")
    wq = nc.dram_tensor("wq", [D, D], F32, kind="ExternalInput")
    wk = nc.dram_tensor("wk", [D, D], F32, kind="ExternalInput")
    wv = nc.dram_tensor("wv", [D, D], F32, kind="ExternalInput")
    wo = nc.dram_tensor("wo", [D, D], F32, kind="ExternalInput")
    w1p = nc.dram_tensor("w1p", [352256, 64], I32, kind="ExternalInput")
    w2p = nc.dram_tensor("w2p", [352256, 64], I32, kind="ExternalInput")
    w3p = nc.dram_tensor("w3p", [352256, 64], I32, kind="ExternalInput")
    s1 = nc.dram_tensor("s1", [704512], F32, kind="ExternalInput")
    s2 = nc.dram_tensor("s2", [704512], F32, kind="ExternalInput")
    s3 = nc.dram_tensor("s3", [704512], F32, kind="ExternalInput")
    y = nc.dram_tensor("y", [T, D], F32, kind="ExternalOutput")

    dbg = "ExternalOutput" if debug else "Internal"
    bq = nc.dram_tensor("bq", [D, D], BF16)
    bk = nc.dram_tensor("bk", [D, D], BF16)
    bv = nc.dram_tensor("bv", [D, D], BF16)
    bo = nc.dram_tensor("bo", [D, D], BF16)
    bf1 = nc.dram_tensor("bf1", [FF, D], BF16, kind=dbg)
    bf2 = nc.dram_tensor("bf2", [D, FF], BF16)
    bf3 = nc.dram_tensor("bf3", [FF, D], BF16)
    tq = nc.dram_tensor("tq", [H * 524288], BF16)
    tk = nc.dram_tensor("tk", [H * 524288], BF16)
    tv = nc.dram_tensor("tv", [H * 524288], BF16)
    to_ = nc.dram_tensor("to_", [DC * 524288], BF16)
    t1 = nc.dram_tensor("t1", [FC * 524288], BF16)
    t3 = nc.dram_tensor("t3", [FC * 524288], BF16)
    t2 = nc.dram_tensor("t2", [DC * 1409024], BF16)
    ln1t = nc.dram_tensor("ln1t", [T, D], BF16, kind=dbg)
    vtmp = nc.dram_tensor("vtmp", [D, T], BF16)
    qs = nc.dram_tensor("qs", [D, T], BF16, kind=dbg)
    kvo = nc.dram_tensor("kvo", [8192, 2048], BF16)
    kvf = nc.dram_tensor("kvf", [16384, 2048], BF16)
    ao = nc.dram_tensor("ao", [D, T], BF16, kind=dbg)
    x1d = nc.dram_tensor("x1d", [T, D], F32, kind=dbg)
    ln2t = nc.dram_tensor("ln2t", [T, D], BF16, kind=dbg)
    ffo = nc.dram_tensor("ffo", [D, T], BF16, kind=dbg)

    p = Prog()

    with contextlib.ExitStack() as es:
        A1 = es.enter_context(nc.sbuf_tensor("A1", [128, 65536], BF16))
        A2 = es.enter_context(nc.sbuf_tensor("A2", [128, 32768], BF16))
        PS = es.enter_context(nc.psum_tensor("PS", [128, 4096], F32))
        ones_bf = es.enter_context(nc.sbuf_tensor("ones_bf", [128, 1], BF16))
        ones_f = es.enter_context(nc.sbuf_tensor("ones_f", [1, 128], F32))
        sc0 = es.enter_context(nc.sbuf_tensor("sc0", [128, 172], F32))
        sc1 = es.enter_context(nc.sbuf_tensor("sc1", [128, 172], F32))
        srecip = es.enter_context(nc.sbuf_tensor("srecip", [1, 1024], F32))
        st_ = [es.enter_context(nc.sbuf_tensor(f"st{i}", [128, 1], F32))
               for i in range(12)]

        A1f = A1.bitcast(F32)
        A1i = A1.bitcast(I32)
        A2f = A2.bitcast(F32)

        for name in ("ld", "st", "cc", "t", "v", "s"):
            p.sem(name, es.enter_context(nc.semaphore(name)))

        def bank(i):
            return PS[:, 512 * i:512 * (i + 1)]

        p.do("vector", "v", 1, lambda e: e.memset(ones_bf[:], 1.0))
        p.do("vector", "v", 1, lambda e: e.memset(ones_f[:], 1.0))

        def cast_transpose_attn(wsrc, bdst, tdst):
            cw32 = [Col(A1f, 0, 4096), Col(A1f, 4096, 4096)]
            cwb = [Col(A1, 16384, 4096), Col(A1, 20480, 4096)]
            tt = [Col(A1, 24576, 4096), Col(A1, 28672, 4096)]
            Lv = [0, 0]
            Vv = [0, 0]
            Sv = [0, 0]
            for i in range(DC):
                b = i % 2
                p.wait("sync", "v", Vv[b])
                Lv[b] = p.do("sync", "ld", 16, lambda e, i=i, b=b: e.dma_start(
                    out=cw32[b](), in_=wsrc[i * 128:(i + 1) * 128, :]))
                p.wait("vector", "ld", Lv[b])
                p.wait("vector", "st", Sv[b])
                Vv[b] = p.do("vector", "v", 1, lambda e, b=b: e.tensor_copy(
                    cwb[b](), cw32[b]()))
                p.wait("scalar", "v", Vv[b])
                Sv[b] = p.do("scalar", "st", 16, lambda e, i=i, b=b: e.dma_start(
                    out=bdst[i * 128:(i + 1) * 128, :], in_=cwb[b]()))
            cast_done = p.cnt["st"]
            TLv = [0, 0]
            TSv = [0, 0]
            for dc in range(DC):
                b = dc % 2
                p.wait("sync", "st", cast_done)
                p.wait("sync", "st", TSv[b])
                TLv[b] = p.do("sync", "ld", 16, lambda e, dc=dc, b=b:
                              e.dma_start_transpose(
                                  out=tt[b](),
                                  in_=bdst[0:D, dc * 128:(dc + 1) * 128]))
                p.wait("scalar", "ld", TLv[b])
                TSv[b] = p.do("scalar", "st", 16, lambda e, dc=dc, b=b:
                              e.dma_start(
                    out=bass.AP(tdst, dc * 128,
                                [[4096, 128], [524288, 32], [1, 128]]),
                    in_=tt[b]().rearrange("p (c m) -> p c m", c=32)))

        def ln_tile(xin_col, lnout_col, sset, sqscr_col):
            rsum, rsq, sm, svar, srstd, sbias = sset

            def step(fn):
                v = p.do("vector", "v", 1, fn)
                p.wait("vector", "v", v)
                return v

            step(lambda e: e.reduce_sum(rsum[:], xin_col(), axis=AX.X))
            sq = p.do("scalar", "s", 1, lambda e: e.activation(
                out=sqscr_col(), in_=xin_col(), func=AF.Square,
                accum_out=rsq[:]))
            p.wait("vector", "s", sq)
            step(lambda e: e.tensor_scalar_mul(sm[:], rsum[:], INV_D))
            step(lambda e: e.tensor_scalar_mul(svar[:], rsq[:], INV_D))
            step(lambda e: e.scalar_tensor_tensor(
                out=svar[:], in0=sm[:], scalar=sm[:], in1=svar[:],
                op0=ALU.mult, op1=ALU.subtract))
            step(lambda e: e.tensor_scalar(
                out=svar[:], in0=svar[:], scalar1=-1.0, scalar2=EPS,
                op0=ALU.mult, op1=ALU.add))
            vv = step(lambda e: e.reciprocal(svar[:], svar[:]))
            p.wait("scalar", "v", vv)
            sv = p.do("scalar", "s", 1, lambda e: e.activation(
                out=srstd[:], in_=svar[:], func=AF.Sqrt))
            p.wait("vector", "s", sv)
            bv_ = step(lambda e: e.tensor_scalar(
                out=sbias[:], in0=sm[:], scalar1=srstd[:], scalar2=-1.0,
                op0=ALU.mult, op1=ALU.mult))
            p.wait("scalar", "v", bv_)
            return p.do("scalar", "s", 1, lambda e: e.activation(
                out=lnout_col(), in_=xin_col(), func=AF.Identity,
                bias=sbias[:], scale=srstd[:]))


        def ln1_pass():
            xt = [Col(A1f, 0, 4096), Col(A1f, 4096, 4096)]
            sqs = Col(A1f, 8192, 4096)
            lnb = [Col(A1, 24576, 4096), Col(A1, 28672, 4096)]
            Sl = [0, 0]
            St = [0, 0]
            for i in range(TT):
                b = i % 2
                p.wait("sync", "s", Sl[b])
                lv = p.do("sync", "ld", 16, lambda e, i=i, b=b: e.dma_start(
                    out=xt[b](), in_=x[i * 128:(i + 1) * 128, :]))
                p.wait("vector", "ld", lv)
                p.wait("scalar", "ld", lv)
                p.wait("scalar", "st", St[b])
                Sl[b] = ln_tile(xt[b], lnb[b], st_[6 * b:6 * b + 6], sqs)
                p.wait("scalar", "s", Sl[b])
                St[b] = p.do("scalar", "st", 16, lambda e, i=i, b=b:
                             e.dma_start(
                    out=ln1t[i * 128:(i + 1) * 128, :], in_=lnb[b]()))

        def dequant(wp, sflat, bdst, ntile, nint, ngrp, width):
            pk = [Col(A1i, 0, 5504), Col(A1i, 5504, 5504)]
            msb = Col(A1i, 11008, 5504)
            lsb = Col(A1i, 16512, 5504)
            dq = [Col(A2, 0, 11008), Col(A2, 11008, 11008)]
            scl = [sc0, sc1]
            g2 = ngrp // 2
            Lv = [0, 0]
            Vv = [0, 0]
            Sv = [0, 0]
            for ti in range(ntile):
                b = ti % 2
                p.wait("sync", "v", Vv[b])
                p.do("sync", "ld", 16, lambda e, ti=ti, b=b: e.dma_start(
                    out=pk[b](0, nint), in_=bass.AP(
                        wp, ti * 128 * nint, [[nint, 128], [1, nint]])))
                Lv[b] = p.do("sync", "ld", 16, lambda e, ti=ti, b=b:
                             e.dma_start(
                    out=scl[b][:, 0:ngrp], in_=bass.AP(
                        sflat, ti * 128 * ngrp, [[ngrp, 128], [1, ngrp]])))
                p.wait("vector", "ld", Lv[b])
                p.wait("vector", "st", Sv[b])
                p.emit("vector", lambda e, b=b: e.tensor_single_scalar(
                    out=msb(0, nint), in_=pk[b](0, nint), scalar=4,
                    op=ALU.arith_shift_right))
                ev = p.do("vector", "v", 1, lambda e, b=b: e.tensor_scalar(
                    out=lsb(0, nint), in0=pk[b](0, nint), scalar1=15,
                    scalar2=8, op0=ALU.bitwise_and, op1=ALU.bitwise_xor))
                p.wait("vector", "v", ev)
                p.emit("vector", lambda e, b=b: e.scalar_tensor_tensor(
                    out=dq[b].ap([[128, g2], [1, 64]], 0),
                    in0=msb(0, nint), scalar=0,
                    in1=bass.AP(scl[b], 0, [[172, 128], [2, g2], [0, 64]]),
                    op0=ALU.add, op1=ALU.mult))
                Vv[b] = p.do("vector", "v", 1, lambda e, b=b:
                             e.scalar_tensor_tensor(
                    out=dq[b].ap([[128, g2], [1, 64]], 64),
                    in0=lsb(0, nint), scalar=-8,
                    in1=bass.AP(scl[b], 1, [[172, 128], [2, g2], [0, 64]]),
                    op0=ALU.add, op1=ALU.mult))
                p.wait("scalar", "v", Vv[b])
                Sv[b] = p.do("scalar", "st", 16, lambda e, ti=ti, b=b:
                             e.dma_start(
                    out=bdst[ti * 128:(ti + 1) * 128, 0:width],
                    in_=dq[b](0, width)))

        def tpass(bsrc, tdst, nrow, ncol_chunks, oc_count, oc_stride, p_str):
            tt = [Col(A1, 0, 11008), Col(A1, 11008, 11008)]
            TSv = [0, 0]
            for dc in range(ncol_chunks):
                b = dc % 2
                p.wait("sync", "st", TSv[b])
                tl = p.do("sync", "ld", 16, lambda e, dc=dc, b=b:
                          e.dma_start_transpose(
                    out=tt[b](0, nrow),
                    in_=bsrc[0:nrow, dc * 128:(dc + 1) * 128]))
                p.wait("scalar", "ld", tl)
                TSv[b] = p.do("scalar", "st", 16, lambda e, dc=dc, b=b:
                              e.dma_start(
                    out=bass.AP(tdst, dc * 128,
                                [[p_str, 128], [oc_stride, oc_count],
                                 [1, 128]]),
                    in_=tt[b](0, nrow).rearrange("p (c m) -> p c m",
                                                 c=oc_count)))

        def load_ln1x_fm(bigbuf):
            for dc in range(DC):
                p.do("sync", "ld", 16, lambda e, dc=dc: e.dma_start_transpose(
                    out=bigbuf(dc * 2048, (dc + 1) * 2048),
                    in_=ln1t[0:T, dc * 128:(dc + 1) * 128]))
            return p.cnt["ld"]

        def proj_pass(tsrc, dst_dram, bigbuf, ln_ld):
            wset = [Col(A2, 0, 4096), Col(A2, 4096, 4096)]
            kst = [Col(A2, 8192, 2048), Col(A2, 10240, 2048)]
            Th = [0, 0]
            Vh = [0, 0]
            Sh = [0, 0]
            for h in range(H):
                b = h % 2
                p.wait("sync", "t", Th[b])
                lw = p.do("sync", "ld", 16, lambda e, h=h, b=b: e.dma_start(
                    out=wset[b](),
                    in_=bass.AP(tsrc, h * 524288, [[4096, 128], [1, 4096]])))
                p.wait("tensor", "ld", max(lw, ln_ld))
                p.wait("tensor", "v", Vh[b])
                for tc in range(TCH):
                    bk_ = (h * TCH + tc) % 8
                    for dc in range(DC):
                        fn = (lambda e, b=b, tc=tc, dc=dc, bk_=bk_: e.matmul(
                            bank(bk_), wset[b](dc * 128, dc * 128 + 128),
                            bigbuf(dc * 2048 + tc * 512,
                                   dc * 2048 + tc * 512 + 512),
                            start=(dc == 0), stop=(dc == DC - 1)))
                        if tc == TCH - 1 and dc == DC - 1:
                            Th[b] = p.do("tensor", "t", 1, fn)
                        else:
                            p.emit("tensor", fn)
                p.wait("vector", "t", Th[b])
                p.wait("vector", "st", Sh[b])
                for tc in range(TCH):
                    bk_ = (h * TCH + tc) % 8
                    Vh[b] = p.do("vector", "v", 1,
                                 lambda e, tc=tc, bk_=bk_, b=b: e.tensor_copy(
                        kst[b](tc * 512, tc * 512 + 512), bank(bk_)))
                p.wait("scalar", "v", Vh[b])
                Sh[b] = p.do("scalar", "st", 16, lambda e, h=h, b=b:
                             e.dma_start(
                    out=bass.AP(dst_dram, h * 128 * 2048,
                                [[2048, 128], [1, 2048]]),
                    in_=kst[b]()))

        def vtpass():
            tt = [Col(A2, 0, 4096), Col(A2, 4096, 4096)]
            TSv = [0, 0]
            for j in range(16):
                b = j % 2
                p.wait("sync", "st", TSv[b])
                tl = p.do("sync", "ld", 16, lambda e, j=j, b=b:
                          e.dma_start_transpose(
                    out=tt[b](), in_=vtmp[0:D, j * 128:(j + 1) * 128]))
                p.wait("scalar", "ld", tl)
                TSv[b] = p.do("scalar", "st", 16, lambda e, j=j, b=b:
                              e.dma_start(
                    out=bass.AP(kvo, VB + j * 128,
                                [[2048, 128], [262144, 32], [1, 128]]),
                    in_=tt[b]().rearrange("p (c m) -> p c m", c=32)))

        def attention(bigbuf):
            qh = [Col(A2, 0, 2048), Col(A2, 2048, 2048)]
            kh = [Col(A2, 4096, 4096), Col(A2, 8192, 4096)]
            vh = [Col(A2, 12288, 4096), Col(A2, 16384, 4096)]
            ex = [Col(A2, 20480, 1024), Col(A2, 21504, 1024)]
            sbc = Col(A2f, 11264, 1024)

            def pst(i, a, b_):
                return PS[:, i * 1024 + a:i * 1024 + b_]

            def po(a, b_):
                return PS[:, 2048 + a:2048 + b_]

            def psm(a, b_):
                return PS[0:1, 3072 + a:3072 + b_]

            Tep = [0, 0]
            last_epi_s = 0
            last_ot_v = 2
            for h in range(H):
                b = h % 2
                p.wait("sync", "t", Tep[b])
                p.do("sync", "ld", 16, lambda e, h=h, b=b: e.dma_start(
                    out=qh[b](), in_=bass.AP(
                        qs, h * 128 * 2048, [[2048, 128], [1, 2048]])))
                for r in range(2):
                    p.do("sync", "ld", 16, lambda e, h=h, b=b, r=r:
                         e.dma_start(
                        out=kh[b](r * 2048, (r + 1) * 2048),
                        in_=bass.AP(kvf, r * RB + h * 128 * 2048,
                                    [[2048, 128], [1, 2048]])))
                lh = 0
                for r in range(2):
                    lh = p.do("sync", "ld", 16, lambda e, h=h, b=b, r=r:
                              e.dma_start(
                        out=vh[b](r * 2048, (r + 1) * 2048),
                        in_=bass.AP(kvf, r * RB + VB + h * 262144,
                                    [[2048, 128], [1, 2048]])))
                for qi in range(2):
                    Texp = {}
                    Tpvk = {}
                    Tdone = 0

                    def scores(kc):
                        eb = kc % 2
                        p.wait("tensor", "ld", lh)
                        if kc >= 2:
                            p.wait("tensor", "s", Texp[kc - 2])
                        else:
                            p.wait("tensor", "s", last_epi_s)
                        nonlocal_t = 0
                        for half in range(2):
                            fn = (lambda e, kc=kc, eb=eb, half=half, h=h, b=b,
                                  qi=qi: e.matmul(
                                pst(eb, half * 512, half * 512 + 512),
                                kh[b](kc * 128, (kc + 1) * 128),
                                qh[b](qi * 1024 + half * 512,
                                      qi * 1024 + half * 512 + 512),
                                start=True, stop=True))
                            if half == 1:
                                nonlocal_t = p.do("tensor", "t", 1, fn)
                            else:
                                p.emit("tensor", fn)
                        p.wait("scalar", "t", nonlocal_t)
                        if kc >= 2:
                            p.wait("scalar", "t", Tpvk.get(kc - 2, 0))
                        Texp[kc] = p.do("scalar", "s", 1,
                                        lambda e, eb=eb: e.activation(
                            out=ex[eb](), in_=pst(eb, 0, 1024), func=AF.Exp,
                            scale=ISQ))

                    def pv(kc):
                        eb = kc % 2
                        p.wait("tensor", "s", Texp[kc])
                        if kc == 0:
                            p.wait("tensor", "v", last_ot_v)
                        for half in range(2):
                            p.emit("tensor", lambda e, b=b, kc=kc, eb=eb,
                                   half=half: e.matmul(
                                po(half * 512, half * 512 + 512),
                                vh[b](kc * 128, (kc + 1) * 128),
                                ex[eb](half * 512, half * 512 + 512),
                                start=(kc == 0), stop=(kc == 31)))
                        for half in range(2):
                            fn = (lambda e, kc=kc, eb=eb, half=half: e.matmul(
                                psm(half * 512, half * 512 + 512),
                                ones_bf[:],
                                ex[eb](half * 512, half * 512 + 512),
                                start=(kc == 0), stop=(kc == 31)))
                            if half == 1:
                                Tpvk[kc] = p.do("tensor", "t", 1, fn)
                            else:
                                p.emit("tensor", fn)

                    for kc in range(32):
                        scores(kc)
                        if kc >= 1:
                            pv(kc - 1)
                    pv(31)
                    Tdone = Tpvk[31]
                    p.wait("vector", "t", Tdone)
                    rv = p.do("vector", "v", 1, lambda e: e.reciprocal(
                        srecip[:], psm(0, 1024)))
                    p.wait("tensor", "v", rv)
                    tb_ = 0
                    for half in range(2):
                        fn = (lambda e, half=half: e.matmul(
                            pst(0, half * 512, half * 512 + 512),
                            ones_f[:],
                            srecip[0:1, half * 512:half * 512 + 512],
                            start=True, stop=True))
                        if half == 1:
                            tb_ = p.do("tensor", "t", 1, fn)
                        else:
                            p.emit("tensor", fn)
                    p.wait("scalar", "t", tb_)
                    last_epi_s = p.do("scalar", "s", 1, lambda e: e.copy(
                        sbc(), pst(0, 0, 1024)))
                    p.wait("vector", "s", last_epi_s)
                    last_ot_v = p.do("vector", "v", 1,
                                     lambda e, h=h, qi=qi:
                                     e.scalar_tensor_tensor(
                        out=bigbuf(h * 2048 + qi * 1024,
                                   h * 2048 + qi * 1024 + 1024),
                        in0=po(0, 1024), scalar=1.0, in1=sbc(),
                        op0=ALU.mult, op1=ALU.mult))
                    if qi == 1:
                        Tep[b] = tb_

        def wo_pass(bigbuf):
            woset = [Col(A2, 0, 4096), Col(A2, 4096, 4096)]
            aost = [Col(A2, 8192, 2048), Col(A2, 10240, 2048)]
            To = [0, 0]
            Vo = [0, 0]
            So = [0, 0]
            for oc in range(DC):
                b = oc % 2
                p.wait("sync", "t", To[b])
                lw = p.do("sync", "ld", 16, lambda e, oc=oc, b=b: e.dma_start(
                    out=woset[b](),
                    in_=bass.AP(to_, oc * 524288, [[4096, 128], [1, 4096]])))
                p.wait("tensor", "ld", lw)
                p.wait("tensor", "v", Vo[b])
                for tc in range(TCH):
                    bk_ = (oc * TCH + tc) % 8
                    for hh in range(H):
                        fn = (lambda e, b=b, tc=tc, hh=hh, bk_=bk_: e.matmul(
                            bank(bk_), woset[b](hh * 128, hh * 128 + 128),
                            bigbuf(hh * 2048 + tc * 512,
                                   hh * 2048 + tc * 512 + 512),
                            start=(hh == 0), stop=(hh == H - 1)))
                        if tc == TCH - 1 and hh == H - 1:
                            To[b] = p.do("tensor", "t", 1, fn)
                        else:
                            p.emit("tensor", fn)
                p.wait("vector", "t", To[b])
                p.wait("vector", "st", So[b])
                for tc in range(TCH):
                    bk_ = (oc * TCH + tc) % 8
                    Vo[b] = p.do("vector", "v", 1,
                                 lambda e, tc=tc, bk_=bk_, b=b: e.tensor_copy(
                        aost[b](tc * 512, tc * 512 + 512), bank(bk_)))
                p.wait("scalar", "v", Vo[b])
                So[b] = p.do("scalar", "st", 16, lambda e, oc=oc, b=b:
                             e.dma_start(
                    out=bass.AP(ao, oc * 128 * 2048, [[2048, 128], [1, 2048]]),
                    in_=aost[b]()))

        def residual_ln2():
            xt = [Col(A1f, 0, 4096), Col(A1f, 4096, 4096)]
            x1t = [Col(A1f, 8192, 4096), Col(A1f, 12288, 4096)]
            sqs = Col(A1f, 16384, 4096)
            lnb = [Col(A1, 40960, 4096), Col(A1, 45056, 4096)]
            aot = [Col(A1, 49152, 4096), Col(A1, 53248, 4096)]
            Sl = [0, 0]
            Sx = [0, 0]
            Sn = [0, 0]
            Vx = [0, 0]
            for i in range(TT):
                b = i % 2
                p.wait("sync", "v", Vx[b])
                p.do("sync", "ld", 16, lambda e, i=i, b=b: e.dma_start(
                    out=xt[b](), in_=x[i * 128:(i + 1) * 128, :]))
                lv = p.do("sync", "ld", 16, lambda e, i=i, b=b:
                          e.dma_start_transpose(
                    out=aot[b](), in_=ao[0:D, i * 128:(i + 1) * 128]))
                p.wait("vector", "ld", lv)
                p.wait("vector", "s", Sl[b])
                p.wait("vector", "st", Sx[b])
                Vx[b] = p.do("vector", "v", 1, lambda e, b=b: e.tensor_tensor(
                    out=x1t[b](), in0=xt[b](), in1=aot[b](), op=ALU.add))
                p.wait("scalar", "v", Vx[b])
                Sx[b] = p.do("scalar", "st", 16, lambda e, i=i, b=b:
                             e.dma_start(
                    out=x1d[i * 128:(i + 1) * 128, :], in_=x1t[b]()))
                p.wait("scalar", "st", Sn[b])
                Sl[b] = ln_tile(x1t[b], lnb[b], st_[6 * b:6 * b + 6], sqs)
                p.wait("scalar", "s", Sl[b])
                Sn[b] = p.do("scalar", "st", 16, lambda e, i=i, b=b:
                             e.dma_start(
                    out=ln2t[i * 128:(i + 1) * 128, :], in_=lnb[b]()))

        def ffn():
            acc = Col(A1f, 0, 16384)
            l2g = Col(A1, 32768, 16384)
            hbuf = Col(A1, 49152, 8192)
            fftb = Col(A1, 32768, 16384)
            w1s = [Col(A2, 0, 4096), Col(A2, 4096, 4096)]
            w3s = [Col(A2, 8192, 4096), Col(A2, 12288, 4096)]
            w2s = [Col(A2, 16384, 2048), Col(A2, 18432, 2048)]
            sil = [Col(A2, 20480, 512), Col(A2, 20992, 512)]
            hgroups = []
            g = 0
            while g < FC:
                hgroups.append((g, min(g + 16, FC)))
                g += 16
            Sff = 0
            for tg in range(TCH):
                p.wait("sync", "st", Sff)
                Ll2 = 0
                for dc in range(DC):
                    Ll2 = p.do("sync", "ld", 16, lambda e, tg=tg, dc=dc:
                               e.dma_start_transpose(
                        out=l2g(dc * 512, (dc + 1) * 512),
                        in_=ln2t[tg * 512:(tg + 1) * 512,
                                 dc * 128:(dc + 1) * 128]))
                mz = p.do("vector", "v", 1, lambda e: e.memset(acc(), 0.0))
                Tab = [0, 0]
                Hv = {}
                Tw2 = [0, 0]
                Av = [0, 0]
                w2done = 0
                for g0, g1 in hgroups:
                    for hc in range(g0, g1):
                        b = hc % 2
                        p.wait("sync", "t", Tab[b])
                        p.do("sync", "ld", 16, lambda e, hc=hc, b=b:
                             e.dma_start(
                            out=w1s[b](), in_=bass.AP(
                                t1, hc * 524288, [[4096, 128], [1, 4096]])))
                        lw = p.do("sync", "ld", 16, lambda e, hc=hc, b=b:
                                  e.dma_start(
                            out=w3s[b](), in_=bass.AP(
                                t3, hc * 524288, [[4096, 128], [1, 4096]])))
                        p.wait("tensor", "ld", max(lw, Ll2))
                        p.wait("tensor", "v", Hv.get(hc - 2, 0))
                        pa = bank(b)
                        pb = bank(2 + b)
                        for dc in range(DC):
                            p.emit("tensor", lambda e, b=b, dc=dc, pa=pa:
                                   e.matmul(
                                pa, w1s[b](dc * 128, dc * 128 + 128),
                                l2g(dc * 512, dc * 512 + 512),
                                start=(dc == 0), stop=(dc == DC - 1)))
                        for dc in range(DC):
                            fn = (lambda e, b=b, dc=dc, pb=pb: e.matmul(
                                pb, w3s[b](dc * 128, dc * 128 + 128),
                                l2g(dc * 512, dc * 512 + 512),
                                start=(dc == 0), stop=(dc == DC - 1)))
                            if dc == DC - 1:
                                Tab[b] = p.do("tensor", "t", 1, fn)
                            else:
                                p.emit("tensor", fn)
                        p.wait("scalar", "t", Tab[b])
                        sl = p.do("scalar", "s", 1, lambda e, b=b, pa=pa:
                                  e.activation(out=sil[b](), in_=pa,
                                               func=AF.Silu))
                        p.wait("vector", "s", sl)
                        p.wait("vector", "t", w2done)
                        Hv[hc] = p.do("vector", "v", 1,
                                      lambda e, b=b, hc=hc, g0=g0, pb=pb:
                                      e.tensor_tensor(
                            out=hbuf((hc - g0) * 512, (hc - g0) * 512 + 512),
                            in0=sil[b](), in1=pb, op=ALU.mult))
                    glen = g1 - g0
                    p.wait("vector", "v", mz)
                    for oc in range(DC):
                        ob = oc % 2
                        p.wait("sync", "t", Tw2[ob])
                        lw2 = p.do("sync", "ld", 16,
                                   lambda e, oc=oc, g0=g0, glen=glen, ob=ob:
                                   e.dma_start(
                            out=w2s[ob](0, glen * 128), in_=bass.AP(
                                t2, oc * 1409024 + g0 * 128,
                                [[11008, 128], [1, glen * 128]])))
                        p.wait("tensor", "ld", lw2)
                        p.wait("tensor", "v", Hv[g1 - 1])
                        p.wait("tensor", "v", Av[ob])
                        pw = bank(4 + ob)
                        for k in range(glen):
                            fn = (lambda e, ob=ob, k=k, pw=pw: e.matmul(
                                pw, w2s[ob](k * 128, k * 128 + 128),
                                hbuf(k * 512, k * 512 + 512),
                                start=(k == 0), stop=(k == glen - 1)))
                            if k == glen - 1:
                                Tw2[ob] = p.do("tensor", "t", 1, fn)
                            else:
                                p.emit("tensor", fn)
                        p.wait("vector", "t", Tw2[ob])
                        Av[ob] = p.do("vector", "v", 1,
                                      lambda e, oc=oc, pw=pw: e.tensor_tensor(
                            out=acc(oc * 512, oc * 512 + 512),
                            in0=acc(oc * 512, oc * 512 + 512),
                            in1=pw, op=ALU.add))
                    w2done = max(Tw2[0], Tw2[1])
                p.wait("vector", "v", max(Av[0], Av[1]))
                cv = p.do("vector", "v", 1, lambda e: e.tensor_copy(
                    fftb(), acc()))
                p.wait("scalar", "v", cv)
                Sff = p.do("scalar", "st", 16, lambda e, tg=tg: e.dma_start(
                    out=bass.AP(ffo, tg * 512,
                                [[2048, 128], [128 * 2048, 32], [1, 512]]),
                    in_=fftb().rearrange("p (c m) -> p c m", c=32)))

        def final_residual():
            x1t = [Col(A1f, 0, 4096), Col(A1f, 4096, 4096)]
            ffot = [Col(A1, 16384, 4096), Col(A1, 20480, 4096)]
            yt = [Col(A1f, 12288, 4096), Col(A1f, 24576, 4096)]
            Vy = [0, 0]
            Sy = [0, 0]
            for i in range(TT):
                b = i % 2
                p.wait("sync", "v", Vy[b])
                p.do("sync", "ld", 16, lambda e, i=i, b=b: e.dma_start(
                    out=x1t[b](), in_=x1d[i * 128:(i + 1) * 128, :]))
                lv = p.do("sync", "ld", 16, lambda e, i=i, b=b:
                          e.dma_start_transpose(
                    out=ffot[b](), in_=ffo[0:D, i * 128:(i + 1) * 128]))
                p.wait("vector", "ld", lv)
                p.wait("vector", "st", Sy[b])
                Vy[b] = p.do("vector", "v", 1, lambda e, b=b: e.tensor_tensor(
                    out=yt[b](), in0=x1t[b](), in1=ffot[b](), op=ALU.add))
                p.wait("scalar", "v", Vy[b])
                Sy[b] = p.do("scalar", "st", 16, lambda e, i=i, b=b:
                             e.dma_start(
                    out=y[i * 128:(i + 1) * 128, :], in_=yt[b]()))
            p.wait("scalar", "st", p.cnt["st"])

        bigbuf = Col(A1, 0, 65536)

        cast_transpose_attn(wk, bk, tk)
        p.barrier()
        cast_transpose_attn(wv, bv, tv)
        p.barrier()
        cast_transpose_attn(wq, bq, tq)
        p.barrier()
        ln1_pass()
        p.barrier()
        if upto in ("p1", "p3", "p4", "all"):
            ln_ld = load_ln1x_fm(bigbuf)
            proj_pass(tk, kvo, bigbuf, ln_ld)
            p.barrier()
            proj_pass(tv, vtmp, bigbuf, ln_ld)
            p.barrier()
            vtpass()
            p.barrier()
            cch = p.h["cc"]
            p.inc("cc", 1)
            p.emit("gpsimd", lambda e: e.collective_compute(
                "AllGather", ALU.bypass,
                replica_groups=[[0, 1], [2, 3], [4, 5], [6, 7]],
                ins=[kvo.ap().opt()], outs=[kvf.ap().opt()]).then_inc(cch))
            proj_pass(tq, qs, bigbuf, ln_ld)
            p.barrier()
        if upto in ("p3", "p4", "all"):
            cast_transpose_attn(wo, bo, to_)
            p.barrier()
            dequant(w1p, s1, bf1, FC, 2048, 64, D)
            p.barrier()
            dequant(w3p, s3, bf3, FC, 2048, 64, D)
            p.barrier()
            dequant(w2p, s2, bf2, DC, 5504, 172, FF)
            p.barrier()
            tpass(bf1, t1, FF, DC, FC, 524288, 4096)
            p.barrier()
            tpass(bf3, t3, FF, DC, FC, 524288, 4096)
            p.barrier()
            tpass(bf2, t2, D, FC, DC, 1409024, 11008)
            p.barrier()
            attention(bigbuf)
            p.barrier()
        if upto in ("p4", "all"):
            wo_pass(bigbuf)
            p.barrier()
            residual_ln2()
            p.barrier()
        if upto == "all":
            ffn()
            p.barrier()
            final_residual()
        p.barrier()

        with nc.Block() as block:
            @block.sync
            def _(e):
                for fn in p.q["sync"]:
                    fn(e)

            @block.vector
            def _(e):
                for fn in p.q["vector"]:
                    fn(e)

            @block.scalar
            def _(e):
                for fn in p.q["scalar"]:
                    fn(e)

            @block.tensor
            def _(e):
                for fn in p.q["tensor"]:
                    fn(e)

            @block.gpsimd
            def _(e):
                for fn in p.q["gpsimd"]:
                    fn(e)

    return nc


def kernel(L0_attn_x, L0_attn_Wq, L0_attn_Wk, L0_attn_Wv, L0_attn_Wo,
           L0_ffn_w1, L0_ffn_s1, L0_ffn_w2, L0_ffn_s2, L0_ffn_w3, L0_ffn_s3):
    xf = np.ascontiguousarray(np.asarray(L0_attn_x).reshape(16384, 4096))
    shards = [np.ascontiguousarray(xf[c * 2048:(c + 1) * 2048])
              for c in range(8)]
    base = dict(
        wq=np.ascontiguousarray(L0_attn_Wq),
        wk=np.ascontiguousarray(L0_attn_Wk),
        wv=np.ascontiguousarray(L0_attn_Wv),
        wo=np.ascontiguousarray(L0_attn_Wo),
        w1p=np.ascontiguousarray(L0_ffn_w1),
        w2p=np.ascontiguousarray(L0_ffn_w2),
        w3p=np.ascontiguousarray(L0_ffn_w3),
        s1=np.ascontiguousarray(L0_ffn_s1),
        s2=np.ascontiguousarray(L0_ffn_s2),
        s3=np.ascontiguousarray(L0_ffn_s3),
    )
    nc = build("all", debug=False)
    in_maps = [dict(base, x=shards[c]) for c in range(8)]
    res = run_bass_kernel_spmd(nc, in_maps, core_ids=list(range(8)))
    out = np.concatenate([np.asarray(r["y"]) for r in res.results], axis=0)
    return out.reshape(4, 4096, 4096).astype(np.float32)
